# version 02 (final)
# baseline (speedup 1.0000x reference)
import numpy as np
from contextlib import ExitStack
import concourse.bass as bass
import concourse.mybir as mybir
from concourse.bass_utils import run_bass_kernel_spmd

F32 = mybir.dt.float32
AF = mybir.ActivationFunctionType
ALU = mybir.AluOpType
AX = mybir.AxisListType

B = 2
S = 8192
D = 1024
T = 2048
NCH = T // 128
NH = 16
HD = 64
DFF = 4096
GELU_C = 0.7978845608028654


class Ser:

    def __init__(self, nsem=8):
        self.nsem = nsem
        self.ops = []
        self.extra = []

    def emit(self, eng, fn, inc=1):
        self.ops.append([eng, fn, inc, False, self.extra])
        self.extra = []

    def dma(self, eng, fn):
        self.emit(eng, fn, 16)

    def emit_async(self, eng, fn, inc=1):
        tok = len(self.ops)
        self.ops.append([eng, fn, inc, True, []])
        return tok

    def dma_async(self, eng, fn):
        return self.emit_async(eng, fn, 16)

    def wait(self, tok):
        self.extra.append(tok)


def _ser_build(Sr, block, sems):
    nsem = Sr.nsem
    cum = [0] * nsem
    pts = []
    sched = []
    prev = None
    for i, (eng, fn, inc, is_async, extra) in enumerate(Sr.ops):
        si = i % nsem
        cum[si] += inc
        pts.append((si, cum[si]))
        waits = []
        if prev is not None:
            waits.append(prev)
        for tok in extra:
            waits.append(pts[tok])
        sched.append((eng, fn, waits, si, inc))
        if not is_async:
            prev = (si, cum[si])
    for ename in ["sync", "tensor", "vector", "scalar", "gpsimd"]:
        myops = [s for s in sched if s[0] == ename]
        if not myops:
            continue

        def body(q, myops=myops):
            for (eng, fn, waits, si, inc) in myops:
                best = {}
                for (ws, wv) in waits:
                    if ws not in best or wv > best[ws]:
                        best[ws] = wv
                for ws, wv in best.items():
                    q.wait_ge(sems[ws], wv)
                fn(q).then_inc(sems[si], inc)

        getattr(block, ename)(body)

    def fin(q, pts=pts, n=len(Sr.ops), nsem=nsem):
        final = {}
        for (si, v) in pts:
            if si not in final or v > final[si]:
                final[si] = v
        for si, v in final.items():
            q.wait_ge(sems[si], v)

    block.gpsimd(fin)


_CONSTS = None


def _consts():
    global _CONSTS
    if _CONSTS is None:
        eye = np.eye(128, dtype=np.float32)
        ones = np.ones((128, 128), dtype=np.float32)
        half = D // 2
        theta = 1.0 / 10000.0 ** (np.arange(half, dtype=np.float64) / half)
        pos = np.arange(S, dtype=np.float64)[:, None]
        ang = pos * theta[None, :]
        cosT = np.cos(ang).astype(np.float32)
        sinT = np.sin(ang).astype(np.float32)
        Hmap = np.zeros((16, D), dtype=np.float32)
        for c in range(D):
            Hmap[c // HD, c] = 1.0
        kk = np.arange(128, dtype=np.float32)[:, None]
        qq = np.arange(512, dtype=np.float32)[None, :]
        KQ = np.broadcast_to(kk - qq, (128, 512)).astype(np.float32).copy()
        iota2d = (np.arange(128, dtype=np.float32)[:, None]
                  + 128.0 * np.arange(64, dtype=np.float32)[None, :]).copy()
        _CONSTS = dict(eye=eye, ones=ones, cosT=cosT, sinT=sinT, Hmap=Hmap,
                       KQ=KQ, iota2d=iota2d)
    return _CONSTS


def _mk():
    return bass.Bass("TRN2", target_bir_lowering=False)


def _colload(Sr, dst_col, src_vec, j):
    Sr.dma("sync", lambda q: q.dma_start(
        dst_col, src_vec[128 * j:128 * (j + 1)].rearrange("(a b) -> a b", b=1)))


def _wtrans(Sr, src_hbm, wrow, wT, psT, eye_sb):
    for j in range(8):
        Sr.dma("sync", lambda q, j=j: q.dma_start(wrow[:], src_hbm[128 * j:128 * (j + 1), :]))
        for k in range(8):
            Sr.emit("tensor", lambda q, k=k: q.transpose(psT[:, 0:128], wrow[:, 128 * k:128 * (k + 1)], eye_sb[:]))
            Sr.emit("scalar", lambda q, j=j, k=k: q.copy(wT[k][:, 128 * j:128 * (j + 1)], psT[:, 0:128]))


def build_A():
    nc = _mk()
    x = nc.dram_tensor("x", [T, D], F32, kind="ExternalInput").ap()
    w1 = nc.dram_tensor("w1", [D], F32, kind="ExternalInput").ap()
    cosH = nc.dram_tensor("cosv", [T, 512], F32, kind="ExternalInput").ap()
    sinH = nc.dram_tensor("sinv", [T, 512], F32, kind="ExternalInput").ap()
    eyeH = nc.dram_tensor("eye", [128, 128], F32, kind="ExternalInput").ap()
    onesH = nc.dram_tensor("ones", [128, 128], F32, kind="ExternalInput").ap()
    xn0T = nc.dram_tensor("xn0T", [D, T], F32, kind="ExternalOutput").ap()
    xr1T = nc.dram_tensor("xr1T", [D, T], F32, kind="ExternalOutput").ap()

    with ExitStack() as ctx:
        sb = lambda n, s: ctx.enter_context(nc.sbuf_tensor(n, s, F32))
        ps = lambda n, s: ctx.enter_context(nc.psum_tensor(n, s, F32))
        xt = sb("xt", [128, D]); sq = sb("sq", [128, D]); xn = sb("xn", [128, D])
        xr = sb("xr", [128, D]); t1 = sb("t1", [128, 512]); t2 = sb("t2", [128, 512])
        cosb = sb("cosb", [128, 512]); sinb = sb("sinb", [128, 512])
        col = sb("col", [128, 1]); col2 = sb("col2", [128, 1]); col3 = sb("col3", [128, 1])
        w1row = sb("w1row", [1, D]); w1b = sb("w1b", [128, D])
        eye_sb = sb("eyes", [128, 128]); ones_sb = sb("oness", [128, 128])
        tev = sb("tev", [128, 128])
        cst = sb("cst", [128, 2])
        pT = ps("pT", [128, 512])
        sems = [ctx.enter_context(nc.semaphore(f"s{i}")) for i in range(8)]
        block = ctx.enter_context(nc.Block())
        Sr = Ser(); E, Dm = Sr.emit, Sr.dma

        Dm("sync", lambda q: q.dma_start(eye_sb[:], eyeH[:, :]))
        Dm("sync", lambda q: q.dma_start(ones_sb[:], onesH[:, :]))
        Dm("sync", lambda q: q.dma_start(w1row[:], w1.rearrange("(a b) -> a b", a=1)))
        E("vector", lambda q: q.memset(cst[:, 0:1], 1e-5))
        E("vector", lambda q: q.memset(cst[:, 1:2], 1e-6))
        for h in range(2):
            E("tensor", lambda q, h=h: q.matmul(pT[:, 0:512], ones_sb[0:1, :], w1row[0:1, 512 * h:512 * (h + 1)], start=True, stop=True))
            E("scalar", lambda q, h=h: q.copy(w1b[:, 512 * h:512 * (h + 1)], pT[:, 0:512]))

        for c in range(NCH):
            r = 128 * c
            Dm("sync", lambda q, r=r: q.dma_start(xt[:], x[r:r + 128, :]))
            Dm("sync", lambda q, r=r: q.dma_start(cosb[:], cosH[r:r + 128, :]))
            Dm("sync", lambda q, r=r: q.dma_start(sinb[:], sinH[r:r + 128, :]))
            E("vector", lambda q: q.tensor_reduce(col[:], xt[:], AX.X, ALU.add))
            E("vector", lambda q: q.tensor_scalar(col[:], col[:], 1.0 / D, None, ALU.mult))
            E("scalar", lambda q: q.activation(sq[:], xt[:], AF.Square))
            E("vector", lambda q: q.tensor_reduce(col2[:], sq[:], AX.X, ALU.add))
            E("vector", lambda q: q.tensor_scalar(col2[:], col2[:], 1.0 / D, None, ALU.mult))
            E("scalar", lambda q: q.activation(col3[:], col[:], AF.Square))
            E("vector", lambda q: q.tensor_tensor(col2[:], col2[:], col3[:], ALU.subtract))
            E("scalar", lambda q: q.activation(col2[:], col2[:], AF.Sqrt, bias=cst[:, 0:1]))
            E("vector", lambda q: q.reciprocal(col2[:], col2[:]))
            E("vector", lambda q: q.tensor_scalar(xn[:], xt[:], col[:], col2[:], ALU.subtract, ALU.mult))
            for j in range(8):
                E("tensor", lambda q, j=j: q.transpose(pT[:, 0:128], xn[:, 128 * j:128 * (j + 1)], eye_sb[:]))
                E("scalar", lambda q: q.copy(tev[:], pT[:, 0:128]))
                Dm("gpsimd", lambda q, j=j, r=r: q.dma_start(xn0T[128 * j:128 * (j + 1), r:r + 128], tev[:]))
            E("scalar", lambda q: q.activation(sq[:], xn[:], AF.Square))
            E("vector", lambda q: q.tensor_reduce(col[:], sq[:], AX.X, ALU.add))
            E("scalar", lambda q: q.activation(col[:], col[:], AF.Sqrt, scale=1.0 / D, bias=cst[:, 1:2]))
            E("vector", lambda q: q.reciprocal(col[:], col[:]))
            E("vector", lambda q: q.tensor_tensor(xn[:], xn[:], w1b[:], ALU.mult))
            E("vector", lambda q: q.tensor_scalar(xn[:], xn[:], col[:], None, ALU.mult))
            E("vector", lambda q: q.tensor_tensor(t1[:], xn[:, 0::2], cosb[:], ALU.mult))
            E("vector", lambda q: q.tensor_tensor(t2[:], xn[:, 1::2], sinb[:], ALU.mult))
            E("vector", lambda q: q.tensor_tensor(xr[:, 0:512], t1[:], t2[:], ALU.subtract))
            E("vector", lambda q: q.tensor_tensor(t1[:], xn[:, 0::2], sinb[:], ALU.mult))
            E("vector", lambda q: q.tensor_tensor(t2[:], xn[:, 1::2], cosb[:], ALU.mult))
            E("vector", lambda q: q.tensor_tensor(xr[:, 512:1024], t1[:], t2[:], ALU.add))
            for j in range(8):
                E("tensor", lambda q, j=j: q.transpose(pT[:, 0:128], xr[:, 128 * j:128 * (j + 1)], eye_sb[:]))
                E("scalar", lambda q: q.copy(tev[:], pT[:, 0:128]))
                Dm("gpsimd", lambda q, j=j, r=r: q.dma_start(xr1T[128 * j:128 * (j + 1), r:r + 128], tev[:]))
        _ser_build(Sr, block, sems)
    return nc


def build_B():
    nc = _mk()
    xn0T = nc.dram_tensor("xn0T", [D, T], F32, kind="ExternalInput").ap()
    Wq = nc.dram_tensor("Wq", [D, D], F32, kind="ExternalInput").ap()
    Wv = nc.dram_tensor("Wv", [D, D], F32, kind="ExternalInput").ap()
    Wg = nc.dram_tensor("Wg", [D, D], F32, kind="ExternalInput").ap()
    dec = nc.dram_tensor("dec", [16], F32, kind="ExternalInput").ap()
    HmapH = nc.dram_tensor("Hmap", [16, D], F32, kind="ExternalInput").ap()
    eyeH = nc.dram_tensor("eye", [128, 128], F32, kind="ExternalInput").ap()
    qT = nc.dram_tensor("qT", [D, T], F32, kind="ExternalOutput").ap()
    vTo = nc.dram_tensor("vT", [D, T], F32, kind="ExternalOutput").ap()
    gT = nc.dram_tensor("gT", [D, T], F32, kind="ExternalOutput").ap()
    Eo = nc.dram_tensor("E", [D], F32, kind="ExternalOutput").ap()

    with ExitStack() as ctx:
        sb = lambda n, s: ctx.enter_context(nc.sbuf_tensor(n, s, F32))
        ps = lambda n, s: ctx.enter_context(nc.psum_tensor(n, s, F32))
        xn = [sb(f"xn{j}", [128, T]) for j in range(8)]
        vT = [sb(f"vt{j}", [128, T]) for j in range(8)]
        wrow = sb("wrow", [128, D])
        wT = [sb(f"wT{j}", [128, D]) for j in range(8)]
        eye_sb = sb("eyes", [128, 128])
        hm = sb("hm", [16, D]); dcol = sb("dcol", [16, 1]); lamc = sb("lamc", [128, 8])
        onesT = sb("onesT", [128, T]); lamt = sb("lamt", [128, T]); st = sb("st", [128, T])
        ecols = sb("ecols", [128, 8]); evb = sb("evb", [128, 512])
        pT = ps("pT", [128, 512]); pc = ps("pc", [128, 1])
        sems = [ctx.enter_context(nc.semaphore(f"s{i}")) for i in range(8)]
        block = ctx.enter_context(nc.Block())
        Sr = Ser(); E, Dm = Sr.emit, Sr.dma

        Dm("sync", lambda q: q.dma_start(eye_sb[:], eyeH[:, :]))
        Dm("sync", lambda q: q.dma_start(hm[:], HmapH[:, :]))
        Dm("sync", lambda q: q.dma_start(dcol[:], dec.rearrange("(a b) -> a b", b=1)))
        for j in range(8):
            Dm("sync", lambda q, j=j: q.dma_start(xn[j][:], xn0T[128 * j:128 * (j + 1), :]))
        for j in range(8):
            E("tensor", lambda q, j=j: q.matmul(pc[:], hm[:, 128 * j:128 * (j + 1)], dcol[:], start=True, stop=True))
            E("scalar", lambda q, j=j: q.copy(lamc[:, j:j + 1], pc[:]))

        def proj(W, out_hbm, silu, keep):
            _wtrans(Sr, W, wrow, wT, pT, eye_sb)
            for oc in range(8):
                for ts in range(4):
                    for dcn in range(8):
                        E("tensor", lambda q, oc=oc, ts=ts, dcn=dcn: q.matmul(
                            pT[:, 0:512], wT[dcn][:, 128 * oc:128 * (oc + 1)],
                            xn[dcn][:, 512 * ts:512 * (ts + 1)],
                            start=(dcn == 0), stop=(dcn == 7)))
                    if silu:
                        E("scalar", lambda q: q.activation(evb[:], pT[:, 0:512], AF.Silu))
                    else:
                        E("scalar", lambda q: q.copy(evb[:], pT[:, 0:512]))
                    if keep is not None:
                        E("vector", lambda q, oc=oc, ts=ts: q.tensor_copy(keep[oc][:, 512 * ts:512 * (ts + 1)], evb[:]))
                    Dm("gpsimd", lambda q, oc=oc, ts=ts: q.dma_start(
                        out_hbm[128 * oc:128 * (oc + 1), 512 * ts:512 * (ts + 1)], evb[:]))

        proj(Wv, vTo, False, vT)
        proj(Wq, qT, False, None)
        proj(Wg, gT, True, None)
        E("vector", lambda q: q.memset(onesT[:], 1.0))
        for j in range(8):
            E("vector", lambda q, j=j: q.tensor_scalar(lamt[:], onesT[:], lamc[:, j:j + 1], None, ALU.mult))
            E("vector", lambda q, j=j: q.tensor_tensor_scan(st[:], lamt[:], vT[j][:], 0.0, ALU.mult, ALU.add))
            E("vector", lambda q, j=j: q.tensor_copy(ecols[:, j:j + 1], st[:, T - 1:T]))
        for j in range(8):
            Dm("gpsimd", lambda q, j=j: q.dma_start(
                Eo[128 * j:128 * (j + 1)].rearrange("(a b) -> a b", b=1), ecols[:, j:j + 1]))
        _ser_build(Sr, block, sems)
    return nc


def build_C():
    nc = _mk()
    vTi = nc.dram_tensor("vT", [D, T], F32, kind="ExternalInput").ap()
    qTi = nc.dram_tensor("qT", [D, T], F32, kind="ExternalInput").ap()
    gTi = nc.dram_tensor("gT", [D, T], F32, kind="ExternalInput").ap()
    Eprev = nc.dram_tensor("Eprev", [3, D], F32, kind="ExternalInput").ap()
    dec = nc.dram_tensor("dec", [16], F32, kind="ExternalInput").ap()
    HmapH = nc.dram_tensor("Hmap", [16, D], F32, kind="ExternalInput").ap()
    gam = nc.dram_tensor("gam", [D], F32, kind="ExternalInput").ap()
    bet = nc.dram_tensor("bet", [D], F32, kind="ExternalInput").ap()
    Wo = nc.dram_tensor("Wo", [D, D], F32, kind="ExternalInput").ap()
    eyeH = nc.dram_tensor("eye", [128, 128], F32, kind="ExternalInput").ap()
    onesH = nc.dram_tensor("ones", [128, 128], F32, kind="ExternalInput").ap()
    sTo = nc.dram_tensor("sT", [D, T], F32, kind="ExternalOutput").ap()

    with ExitStack() as ctx:
        sb = lambda n, s: ctx.enter_context(nc.sbuf_tensor(n, s, F32))
        ps = lambda n, s: ctx.enter_context(nc.psum_tensor(n, s, F32))
        stt_ = [sb(f"st{j}", [128, T]) for j in range(8)]
        wrow = sb("wrow", [128, D])
        wT = [sb(f"wT{j}", [128, D]) for j in range(8)]
        eye_sb = sb("eyes", [128, 128]); ones_sb = sb("oness", [128, 128])
        hm = sb("hm", [16, D]); dcol = sb("dcol", [16, 1])
        lamc = sb("lamc", [128, 8]); l2048 = sb("l2048", [128, 8]); l4096 = sb("l4096", [128, 8])
        gcol = sb("gcol", [128, 1]); tcol = sb("tcol", [128, 1]); ecol = sb("ecol", [128, 3])
        buf = sb("buf", [128, T]); lamt = sb("lamt", [128, T]); onesT = sb("onesT", [128, T])
        yg = [sb(f"yg{j}", [128, 512]) for j in range(8)]
        sqb = sb("sqb", [128, 512])
        murow = sb("murow", [1, 512]); ssrow = sb("ssrow", [1, 512]); vrow = sb("vrow", [1, 512])
        mub = sb("mub", [128, 512]); rsb = sb("rsb", [128, 512])
        gcols = sb("gcols", [128, 8]); bcols = sb("bcols", [128, 8])
        evb = sb("evb", [128, 512])
        cst = sb("cst", [128, 2])
        pT = ps("pT", [128, 512]); pr = ps("pr", [1, 512]); pr2 = ps("pr2", [1, 512])
        pc = ps("pc", [128, 1])
        sems = [ctx.enter_context(nc.semaphore(f"s{i}")) for i in range(8)]
        block = ctx.enter_context(nc.Block())
        Sr = Ser(); E, Dm = Sr.emit, Sr.dma

        Dm("sync", lambda q: q.dma_start(eye_sb[:], eyeH[:, :]))
        Dm("sync", lambda q: q.dma_start(ones_sb[:], onesH[:, :]))
        Dm("sync", lambda q: q.dma_start(hm[:], HmapH[:, :]))
        Dm("sync", lambda q: q.dma_start(dcol[:], dec.rearrange("(a b) -> a b", b=1)))
        E("vector", lambda q: q.memset(cst[:, 0:1], 1e-5))
        E("vector", lambda q: q.memset(cst[:, 1:2], 1e-6))
        for j in range(8):
            _colload(Sr, gcols[:, j:j + 1], gam, j)
            _colload(Sr, bcols[:, j:j + 1], bet, j)
        for j in range(8):
            E("tensor", lambda q, j=j: q.matmul(pc[:], hm[:, 128 * j:128 * (j + 1)], dcol[:], start=True, stop=True))
            E("scalar", lambda q, j=j: q.copy(lamc[:, j:j + 1], pc[:]))
        E("vector", lambda q: q.tensor_copy(l2048[:], lamc[:]))
        for _ in range(11):
            E("vector", lambda q: q.tensor_tensor(l2048[:], l2048[:], l2048[:], ALU.mult))
        E("vector", lambda q: q.tensor_tensor(l4096[:], l2048[:], l2048[:], ALU.mult))
        E("vector", lambda q: q.memset(onesT[:], 1.0))
        for j in range(8):
            for m in range(3):
                Dm("sync", lambda q, j=j, m=m: q.dma_start(
                    ecol[:, m:m + 1], Eprev[m, 128 * j:128 * (j + 1)].rearrange("(a b) -> a b", b=1)))
            E("vector", lambda q, j=j: q.tensor_scalar(gcol[:], ecol[:, 1:2], l2048[:, j:j + 1], None, ALU.mult))
            E("vector", lambda q: q.tensor_tensor(gcol[:], gcol[:], ecol[:, 0:1], ALU.add))
            E("vector", lambda q, j=j: q.tensor_scalar(tcol[:], ecol[:, 2:3], l4096[:, j:j + 1], None, ALU.mult))
            E("vector", lambda q: q.tensor_tensor(gcol[:], gcol[:], tcol[:], ALU.add))
            Dm("sync", lambda q, j=j: q.dma_start(buf[:], vTi[128 * j:128 * (j + 1), :]))
            E("vector", lambda q, j=j: q.tensor_scalar(lamt[:], onesT[:], lamc[:, j:j + 1], None, ALU.mult))
            E("vector", lambda q, j=j: q.tensor_tensor_scan(stt_[j][:], lamt[:], buf[:], gcol[:], ALU.mult, ALU.add))
            Dm("sync", lambda q, j=j: q.dma_start(buf[:], qTi[128 * j:128 * (j + 1), :]))
            E("vector", lambda q, j=j: q.tensor_tensor(stt_[j][:], stt_[j][:], buf[:], ALU.mult))
        _wtrans(Sr, Wo, wrow, wT, pT, eye_sb)
        for ts in range(4):
            sl = slice(512 * ts, 512 * (ts + 1))
            for j in range(8):
                E("tensor", lambda q, j=j, sl=sl: q.matmul(pr[:], ones_sb[:, 0:1], stt_[j][:, sl], start=(j == 0), stop=(j == 7)))
            for j in range(8):
                E("scalar", lambda q, j=j, sl=sl: q.activation(sqb[:], stt_[j][:, sl], AF.Square))
                E("tensor", lambda q, j=j: q.matmul(pr2[:], ones_sb[:, 0:1], sqb[:], start=(j == 0), stop=(j == 7)))
            E("scalar", lambda q: q.activation(murow[:], pr[:], AF.Copy, scale=1.0 / D))
            E("scalar", lambda q: q.activation(ssrow[:], pr2[:], AF.Copy, scale=1.0 / D))
            E("scalar", lambda q: q.activation(vrow[:], murow[:], AF.Square))
            E("vector", lambda q: q.tensor_tensor(vrow[:], ssrow[:], vrow[:], ALU.subtract))
            E("scalar", lambda q: q.activation(vrow[:], vrow[:], AF.Sqrt, bias=cst[0:1, 0:1]))
            E("vector", lambda q: q.reciprocal(vrow[:], vrow[:]))
            E("tensor", lambda q: q.matmul(pT[:, 0:512], ones_sb[0:1, :], murow[:], start=True, stop=True))
            E("scalar", lambda q: q.copy(mub[:], pT[:, 0:512]))
            E("tensor", lambda q: q.matmul(pT[:, 0:512], ones_sb[0:1, :], vrow[:], start=True, stop=True))
            E("scalar", lambda q: q.copy(rsb[:], pT[:, 0:512]))
            for j in range(8):
                E("vector", lambda q, j=j, sl=sl: q.tensor_tensor(yg[j][:], stt_[j][:, sl], mub[:], ALU.subtract))
                E("vector", lambda q, j=j: q.tensor_tensor(yg[j][:], yg[j][:], rsb[:], ALU.mult))
                E("vector", lambda q, j=j: q.tensor_scalar(yg[j][:], yg[j][:], gcols[:, j:j + 1], bcols[:, j:j + 1], ALU.mult, ALU.add))
                Dm("sync", lambda q, j=j, sl=sl: q.dma_start(sqb[:], gTi[128 * j:128 * (j + 1), sl]))
                E("vector", lambda q, j=j: q.tensor_tensor(yg[j][:], yg[j][:], sqb[:], ALU.mult))
            for oc in range(8):
                for j in range(8):
                    E("tensor", lambda q, oc=oc, j=j: q.matmul(pT[:, 0:512], wT[j][:, 128 * oc:128 * (oc + 1)], yg[j][:], start=(j == 0), stop=(j == 7)))
                E("scalar", lambda q: q.copy(evb[:], pT[:, 0:512]))
                Dm("gpsimd", lambda q, oc=oc, sl=sl: q.dma_start(sTo[128 * oc:128 * (oc + 1), sl], evb[:]))
        _ser_build(Sr, block, sems)
    return nc


def build_D(cls):
    npre = T * (cls + 1)
    nkc = npre // 128
    nsl = npre // 512
    nc = _mk()
    xrB = nc.dram_tensor("xrB", [D, npre], F32, kind="ExternalInput").ap()
    xn0T = nc.dram_tensor("xn0T", [D, T], F32, kind="ExternalInput").ap()
    Wq = nc.dram_tensor("Wq", [D, D], F32, kind="ExternalInput").ap()
    Wk = nc.dram_tensor("Wk", [D, 256], F32, kind="ExternalInput").ap()
    Wv = nc.dram_tensor("Wv", [D, 256], F32, kind="ExternalInput").ap()
    Wo = nc.dram_tensor("Wo", [D, D], F32, kind="ExternalInput").ap()
    w2 = nc.dram_tensor("w2", [D], F32, kind="ExternalInput").ap()
    rw = nc.dram_tensor("rw", [D, 8], F32, kind="ExternalInput").ap()
    rb = nc.dram_tensor("rb", [8], F32, kind="ExternalInput").ap()
    KQH = nc.dram_tensor("KQ", [128, 512], F32, kind="ExternalInput").ap()
    eyeH = nc.dram_tensor("eye", [128, 128], F32, kind="ExternalInput").ap()
    onesH = nc.dram_tensor("ones", [128, 128], F32, kind="ExternalInput").ap()
    hmidT = nc.dram_tensor("hmidT", [D, T], F32, kind="ExternalOutput").ap()
    xr2T = nc.dram_tensor("xr2T", [D, T], F32, kind="ExternalOutput").ap()
    combT = nc.dram_tensor("combT", [8, T], F32, kind="ExternalOutput").ap()

    with ExitStack() as ctx:
        sb = lambda n, s: ctx.enter_context(nc.sbuf_tensor(n, s, F32))
        ps = lambda n, s: ctx.enter_context(nc.psum_tensor(n, s, F32))
        KTg = [sb(f"KTg{j}", [64, npre]) for j in range(2)]
        Vph = sb("Vph", [128, 130 * nkc])
        wvh = sb("wvh", [128, 128 * 8])
        qq8 = [sb(f"qq{j}", [64, 512]) for j in range(8)]
        ha = [sb(f"ha{j}", [128, T]) for j in range(8)]
        xsl = [sb(f"xsl{j}", [128, 512]) for j in range(8)]
        wbuf = sb("wbuf", [128, 128])
        kqc = sb("kqc", [128, 512])
        eye_sb = sb("eyes", [128, 128]); ones_sb = sb("oness", [128, 128])
        attnT = sb("attnT", [128, 512]); scr = sb("scr", [128, 512])
        rrow = sb("rrow", [1, 512])
        w2c = sb("w2c", [128, 8])
        rwc = [sb(f"rwc{j}", [128, 8]) for j in range(8)]
        rbb = sb("rbb", [128, 8]); rbrow = sb("rbrow", [1, 8])
        lg = sb("lg", [128, 8])
        m1 = sb("m1", [128, 1]); m2 = sb("m2", [128, 1])
        mskl = sb("mskl", [128, 8]); numl = sb("numl", [128, 8]); den = sb("den", [128, 1])
        cst = sb("cst", [128, 2])
        pqk = ps("pqk", [128, 512]); pav = ps("pav", [65, 512]); pbc = ps("pbc", [64, 512])
        pmm = ps("pmm", [128, 512]); prow = ps("prow", [1, 512]); psm = ps("psm", [128, 128])
        sems = [ctx.enter_context(nc.semaphore(f"s{i}")) for i in range(8)]
        block = ctx.enter_context(nc.Block())
        Sr = Ser(); E, Dm = Sr.emit, Sr.dma

        Dm("sync", lambda q: q.dma_start(eye_sb[:], eyeH[:, :]))
        Dm("sync", lambda q: q.dma_start(ones_sb[:], onesH[:, :]))
        Dm("sync", lambda q: q.dma_start(kqc[:], KQH[:, :]))
        E("vector", lambda q: q.memset(cst[:, 0:1], 1e-5))
        E("vector", lambda q: q.memset(cst[:, 1:2], 1e-6))
        for j in range(8):
            Dm("sync", lambda q, j=j: q.dma_start(rwc[j][:], rw[128 * j:128 * (j + 1), :]))
            _colload(Sr, w2c[:, j:j + 1], w2, j)
        Dm("sync", lambda q: q.dma_start(rbrow[:], rb.rearrange("(a b) -> a b", a=1)))
        E("tensor", lambda q: q.matmul(pmm[:, 0:8], ones_sb[0:1, :], rbrow[:], start=True, stop=True))
        E("scalar", lambda q: q.copy(rbb[:], pmm[:, 0:8]))

        for gh in range(2):
            E("vector", lambda q: q.memset(Vph[:], 1.0))
            Dm("sync", lambda q, gh=gh: q.dma_start(
                wvh[:].rearrange("p (j f) -> p j f", j=8),
                Wv[:, 128 * gh:128 * (gh + 1)].rearrange("(j p) f -> p j f", p=128)))
            for ks in range(nsl):
                sl = slice(512 * ks, 512 * (ks + 1))
                for j in range(8):
                    Dm("sync", lambda q, j=j, sl=sl: q.dma_start(xsl[j][:], xrB[128 * j:128 * (j + 1), sl]))
                for j in range(8):
                    Dm("sync", lambda q, j=j, gh=gh: q.dma_start(wbuf[:], Wk[128 * j:128 * (j + 1), 128 * gh:128 * (gh + 1)]))
                    E("tensor", lambda q, j=j: q.matmul(pqk[:, 0:512], wbuf[:], xsl[j][:], start=(j == 0), stop=(j == 7)))
                E("scalar", lambda q, sl=sl: q.copy(KTg[0][:, sl], pqk[0:64, 0:512]))
                E("scalar", lambda q, sl=sl: q.copy(KTg[1][:, sl], pqk[64:128, 0:512]))
                for kk in range(4):
                    kc = 4 * ks + kk
                    for j in range(8):
                        E("tensor", lambda q, j=j, kk=kk: q.matmul(pmm[:, 0:128], xsl[j][:, 128 * kk:128 * (kk + 1)], wvh[:, 128 * j:128 * (j + 1)], start=(j == 0), stop=(j == 7)))
                    for gl in range(2):
                        E("vector", lambda q, kc=kc, gl=gl: q.tensor_copy(Vph[:, 130 * kc + 65 * gl:130 * kc + 65 * gl + 64], pmm[:, 64 * gl:64 * (gl + 1)]))
            for qt_i in range(4):
                for j in range(8):
                    Dm("sync", lambda q, j=j, qt_i=qt_i, cls=cls: q.dma_start(
                        xsl[j][:], xrB[128 * j:128 * (j + 1), 2048 * cls + 512 * qt_i:2048 * cls + 512 * (qt_i + 1)]))
                for ocl in range(4):
                    oc = 4 * gh + ocl
                    for j in range(8):
                        Dm("sync", lambda q, j=j, oc=oc: q.dma_start(wbuf[:], Wq[128 * j:128 * (j + 1), 128 * oc:128 * (oc + 1)]))
                        E("tensor", lambda q, j=j: q.matmul(pqk[:, 0:512], wbuf[:], xsl[j][:], start=(j == 0), stop=(j == 7)))
                    E("scalar", lambda q, ocl=ocl: q.copy(qq8[2 * ocl][:], pqk[0:64, 0:512]))
                    E("scalar", lambda q, ocl=ocl: q.copy(qq8[2 * ocl + 1][:], pqk[64:128, 0:512]))
                q0c = 16 * cls + 4 * qt_i
                nk = q0c + 4
                for hl in range(8):
                    h = 8 * gh + hl
                    gl = hl // 4
                    for kc in range(nk):
                        E("tensor", lambda q, gl=gl, kc=kc, hl=hl: q.matmul(
                            pqk[:, 0:512], KTg[gl][:, 128 * kc:128 * (kc + 1)],
                            qq8[hl][:], start=True, stop=True))
                        jrel = kc - q0c
                        if jrel >= 0:
                            E("vector", lambda q, jrel=jrel: q.tensor_scalar(scr[:], kqc[:], 128.0 * jrel, None, ALU.add))
                            E("vector", lambda q: q.tensor_scalar(scr[:], scr[:], 0.0, -1e9, ALU.is_gt, ALU.mult))
                            E("vector", lambda q: q.scalar_tensor_tensor(scr[:], pqk[:, 0:512], 0.125, scr[:], ALU.mult, ALU.add))
                            E("scalar", lambda q: q.activation(attnT[:], scr[:], AF.Exp))
                        else:
                            E("scalar", lambda q: q.activation(attnT[:], pqk[:, 0:512], AF.Exp, scale=0.125))
                        E("tensor", lambda q, kc=kc, gl=gl, nk=nk: q.matmul(
                            pav[:, 0:512], Vph[:, 130 * kc + 65 * gl:130 * kc + 65 * gl + 65],
                            attnT[:], start=(kc == 0), stop=(kc == nk - 1)))
                    E("vector", lambda q: q.reciprocal(rrow[:], pav[64:65, :]))
                    E("tensor", lambda q: q.matmul(pbc[:, 0:512], ones_sb[0:1, 0:64], rrow[:], start=True, stop=True))
                    E("scalar", lambda q: q.copy(scr[0:64, :], pav[0:64, :]))
                    hb = 64 * (h % 2)
                    E("vector", lambda q, h=h, hb=hb, qt_i=qt_i: q.tensor_tensor(
                        ha[h // 2][hb:hb + 64, 512 * qt_i:512 * (qt_i + 1)], scr[0:64, :], pbc[:, 0:512], ALU.mult))
        for qt_i in range(4):
            tsl = slice(512 * qt_i, 512 * (qt_i + 1))
            for j in range(8):
                Dm("sync", lambda q, j=j, tsl=tsl: q.dma_start(xsl[j][:], xn0T[128 * j:128 * (j + 1), tsl]))
            for oc in range(8):
                for j in range(8):
                    Dm("sync", lambda q, j=j, oc=oc: q.dma_start(wbuf[:], Wo[128 * j:128 * (j + 1), 128 * oc:128 * (oc + 1)]))
                    E("tensor", lambda q, j=j, tsl=tsl: q.matmul(pqk[:, 0:512], wbuf[:], ha[j][:, tsl], start=(j == 0), stop=(j == 7)))
                E("vector", lambda q, oc=oc: q.tensor_tensor(xsl[oc][:], xsl[oc][:], pqk[:, 0:512], ALU.add))
                Dm("gpsimd", lambda q, oc=oc, tsl=tsl: q.dma_start(hmidT[128 * oc:128 * (oc + 1), tsl], xsl[oc][:]))
            for j in range(8):
                E("scalar", lambda q, j=j: q.activation(scr[:], xsl[j][:], AF.Square))
                E("tensor", lambda q, j=j: q.matmul(prow[:], ones_sb[:, 0:1], scr[:], start=(j == 0), stop=(j == 7)))
            E("scalar", lambda q: q.activation(rrow[:], prow[:], AF.Sqrt, scale=1.0 / D, bias=cst[0:1, 1:2]))
            E("vector", lambda q: q.reciprocal(rrow[:], rrow[:]))
            E("tensor", lambda q: q.matmul(pqk[:, 0:512], ones_sb[0:1, :], rrow[:], start=True, stop=True))
            E("scalar", lambda q: q.copy(scr[:], pqk[:, 0:512]))
            for j in range(8):
                E("vector", lambda q, j=j: q.tensor_scalar(xsl[j][:], xsl[j][:], w2c[:, j:j + 1], None, ALU.mult))
                E("vector", lambda q, j=j: q.tensor_tensor(xsl[j][:], xsl[j][:], scr[:], ALU.mult))
                Dm("gpsimd", lambda q, j=j, tsl=tsl: q.dma_start(xr2T[128 * j:128 * (j + 1), tsl], xsl[j][:]))
            for j in range(8):
                E("tensor", lambda q, j=j: q.matmul(pav[0:8, 0:512], rwc[j][:], xsl[j][:], start=(j == 0), stop=(j == 7)))
            E("scalar", lambda q: q.copy(attnT[0:8, :], pav[0:8, 0:512]))
            for sub in range(4):
                E("tensor", lambda q, sub=sub: q.transpose(psm[0:128, 0:8], attnT[0:8, 128 * sub:128 * (sub + 1)], eye_sb[0:8, 0:8]))
                E("vector", lambda q: q.tensor_tensor(lg[:], psm[0:128, 0:8], rbb[:], ALU.add))
                E("vector", lambda q: q.tensor_reduce(m1[:], lg[:], AX.X, ALU.max))
                E("vector", lambda q: q.tensor_scalar(mskl[:], lg[:], m1[:], None, ALU.is_ge))
                E("vector", lambda q: q.scalar_tensor_tensor(numl[:], mskl[:], -1e30, lg[:], ALU.mult, ALU.add))
                E("vector", lambda q: q.tensor_reduce(m2[:], numl[:], AX.X, ALU.max))
                E("vector", lambda q: q.tensor_scalar(den[:], m1[:], -1.0, None, ALU.mult))
                E("scalar", lambda q: q.activation(numl[:], lg[:], AF.Exp, bias=den[:]))
                E("vector", lambda q: q.tensor_tensor(den[:], m2[:], m1[:], ALU.subtract))
                E("scalar", lambda q: q.activation(den[:], den[:], AF.Exp))
                E("vector", lambda q: q.tensor_scalar(den[:], den[:], 1.0, None, ALU.add))
                E("vector", lambda q: q.reciprocal(den[:], den[:]))
                E("vector", lambda q: q.tensor_scalar(mskl[:], lg[:], m2[:], None, ALU.is_ge))
                E("vector", lambda q: q.tensor_tensor(numl[:], numl[:], mskl[:], ALU.mult))
                E("vector", lambda q: q.tensor_scalar(numl[:], numl[:], den[:], None, ALU.mult))
                E("tensor", lambda q: q.transpose(psm[0:8, 0:128], numl[:], eye_sb[:]))
                E("scalar", lambda q, sub=sub: q.copy(attnT[0:8, 128 * sub:128 * (sub + 1)], psm[0:8, 0:128]))
            Dm("gpsimd", lambda q, qt_i=qt_i: q.dma_start(combT[:, 512 * qt_i:512 * (qt_i + 1)], attnT[0:8, :]))
        _ser_build(Sr, block, sems)
    return nc


def build_E():
    nc = _mk()
    xr2T = nc.dram_tensor("xr2T", [D, T], F32, kind="ExternalInput").ap()
    comb2 = nc.dram_tensor("comb2", [2, T], F32, kind="ExternalInput").ap()
    W1 = nc.dram_tensor("W1", [2, D, DFF], F32, kind="ExternalInput").ap()
    W2 = nc.dram_tensor("W2", [2, D, DFF], F32, kind="ExternalInput").ap()
    W3 = nc.dram_tensor("W3", [2, DFF, D], F32, kind="ExternalInput").ap()
    b1 = nc.dram_tensor("b1", [2, DFF], F32, kind="ExternalInput").ap()
    b2 = nc.dram_tensor("b2", [2, DFF], F32, kind="ExternalInput").ap()
    b3 = nc.dram_tensor("b3", [2, D], F32, kind="ExternalInput").ap()
    onesH = nc.dram_tensor("ones", [128, 128], F32, kind="ExternalInput").ap()
    eyeH = nc.dram_tensor("eye", [128, 128], F32, kind="ExternalInput").ap()
    moein = nc.dram_tensor("moein", [D, T], F32, kind="ExternalInput").ap()
    moeout = nc.dram_tensor("moeout", [D, T], F32, kind="ExternalOutput").ap()

    with ExitStack() as ctx:
        sb = lambda n, s: ctx.enter_context(nc.sbuf_tensor(n, s, F32))
        ps = lambda n, s: ctx.enter_context(nc.psum_tensor(n, s, F32))
        xr = [sb(f"xr{j}", [128, T]) for j in range(8)]
        hT = [sb(f"hT{j}", [128, 512]) for j in range(32)]
        xe2 = [sb(f"xe2{j}", [128, 512]) for j in range(8)]
        acc = [sb(f"acc{j}", [128, 512]) for j in range(8)]
        w1b = sb("w1b", [128, 1024]); w1c = sb("w1c", [128, 1024])
        w3b = sb("w3b", [128, 512]); w3c = sb("w3c", [128, 512])
        crow = sb("crow", [1, 512]); cb = sb("cb", [128, 512])
        ones_sb = sb("oness", [128, 128]); eye_sb = sb("eyes", [128, 128])
        b1c = sb("b1c", [128, 64])
        b2s = sb("b2s", [1, 128]); b3s = sb("b3s", [1, 512])
        scr = sb("scr", [128, 512]); minb = sb("minb", [128, 128])
        p0 = ps("p0", [128, 512]); p1 = ps("p1", [128, 512]); p2 = ps("p2", [128, 512])
        p3 = ps("p3", [128, 512]); p4 = ps("p4", [128, 512]); psm = ps("psm", [128, 128])
        sems = [ctx.enter_context(nc.semaphore(f"s{i}")) for i in range(8)]
        block = ctx.enter_context(nc.Block())
        Sr = Ser(); E, Dm = Sr.emit, Sr.dma
        psl = [p0, p1, p2, p3]

        Dm("sync", lambda q: q.dma_start(ones_sb[:], onesH[:, :]))
        Dm("sync", lambda q: q.dma_start(eye_sb[:], eyeH[:, :]))
        for j in range(8):
            Dm("sync", lambda q, j=j: q.dma_start(xr[j][:], xr2T[128 * j:128 * (j + 1), :]))
        for e in range(2):
            for fc in range(32):
                Dm("sync", lambda q, e=e, fc=fc: q.dma_start(
                    b1c[:, 32 * e + fc:32 * e + fc + 1],
                    b1[e, 128 * fc:128 * (fc + 1)].rearrange("(a b) -> a b", b=1)))


        for tb in range(4):
            tsl = slice(512 * tb, 512 * (tb + 1))
            for e in range(2):
                Dm("sync", lambda q, e=e, tsl=tsl: q.dma_start(crow[:], comb2[e:e + 1, tsl]))
                E("tensor", lambda q: q.matmul(p4[:, 0:512], ones_sb[0:1, :], crow[:], start=True, stop=True))
                E("scalar", lambda q: q.copy(cb[:], p4[:, 0:512]))
                for j in range(8):
                    E("vector", lambda q, j=j, tsl=tsl: q.tensor_tensor(xe2[j][:], xr[j][:, tsl], cb[:], ALU.mult))
                bufs = [w1b, w1c]

                def wload(m, e=e):
                    W = W1 if m % 2 == 0 else W2
                    fc = m // 2
                    buf = bufs[m % 2]
                    return Sr.dma_async("sync", lambda q, W=W, fc=fc, buf=buf, e=e: q.dma_start(
                        buf[:].rearrange("p (j f) -> p j f", j=8),
                        W[e, :, 128 * fc:128 * (fc + 1)].rearrange("(j p) f -> p j f", p=128)))

                toks = {0: wload(0)}
                for fc in range(32):
                    m1i = 2 * fc
                    m2i = 2 * fc + 1
                    toks[m2i] = wload(m2i)
                    Sr.wait(toks.pop(m1i))
                    for j in range(8):
                        E("tensor", lambda q, j=j, tsl=tsl, b=bufs[m1i % 2]: q.matmul(p0[:, 0:512], b[:, 128 * j:128 * (j + 1)], xr[j][:, tsl], start=(j == 0), stop=(j == 7)))
                    if fc + 1 < 32:
                        toks[m2i + 1] = wload(m2i + 1)
                    Dm("sync", lambda q, e=e, fc=fc: q.dma_start(b2s[:], b2[e, 128 * fc:128 * (fc + 1)].rearrange("(a b) -> a b", a=1)))
                    Sr.wait(toks.pop(m2i))
                    for j in range(8):
                        E("tensor", lambda q, j=j, b=bufs[m2i % 2]: q.matmul(p1[:, 0:512], b[:, 128 * j:128 * (j + 1)], xe2[j][:], start=(j == 0), stop=False))
                    E("tensor", lambda q: q.matmul(p1[:, 0:512], b2s[:], crow[:], start=False, stop=True))
                    E("scalar", lambda q, e=e, fc=fc: q.activation(scr[:], p0[:, 0:512], AF.Silu, bias=b1c[:, 32 * e + fc:32 * e + fc + 1]))
                    E("vector", lambda q, fc=fc: q.tensor_tensor(hT[fc][:], p1[:, 0:512], scr[:], ALU.mult))
                w3bufs = [w3b, w3c]

                def w3load(m, e=e):
                    half, fc = divmod(m, 32)
                    return Sr.dma_async("sync", lambda q, fc=fc, half=half, buf=w3bufs[m % 2], e=e: q.dma_start(
                        buf[:], W3[e, 128 * fc:128 * (fc + 1), 512 * half:512 * (half + 1)]))

                t3 = {0: w3load(0)}
                for half in range(2):
                    Dm("sync", lambda q, e=e, half=half: q.dma_start(b3s[:], b3[e, 512 * half:512 * (half + 1)].rearrange("(a b) -> a b", a=1)))
                    for fc in range(32):
                        m = 32 * half + fc
                        if m + 1 < 64:
                            t3[m + 1] = w3load(m + 1)
                        Sr.wait(t3.pop(m))
                        for tt in range(4):
                            E("tensor", lambda q, fc=fc, tt=tt, b=w3bufs[m % 2]: q.matmul(
                                psl[tt][:, 0:512], hT[fc][:, 128 * tt:128 * (tt + 1)], b[:], start=(fc == 0), stop=False))
                    for tt in range(4):
                        E("tensor", lambda q, tt=tt: q.matmul(
                            psl[tt][:, 0:512], crow[0:1, 128 * tt:128 * (tt + 1)], b3s[:], start=False, stop=True))
                    for tt in range(4):
                        if e == 0:
                            E("scalar", lambda q, tt=tt, half=half: q.copy(acc[4 * half + tt][:], psl[tt][:, 0:512]))
                        else:
                            E("vector", lambda q, tt=tt, half=half: q.tensor_tensor(acc[4 * half + tt][:], acc[4 * half + tt][:], psl[tt][:, 0:512], ALU.add))
            for half in range(2):
                for tt in range(4):
                    for ocs in range(4):
                        jt = 4 * half + ocs
                        tcol0 = 512 * tb + 128 * tt
                        E("tensor", lambda q, half=half, tt=tt, ocs=ocs: q.transpose(
                            psm[:, 0:128], acc[4 * half + tt][:, 128 * ocs:128 * (ocs + 1)], eye_sb[:]))
                        Dm("sync", lambda q, jt=jt, tcol0=tcol0: q.dma_start(minb[:], moein[128 * jt:128 * (jt + 1), tcol0:tcol0 + 128]))
                        E("vector", lambda q: q.tensor_tensor(minb[:], minb[:], psm[:, 0:128], ALU.add))
                        Dm("gpsimd", lambda q, jt=jt, tcol0=tcol0: q.dma_start(moeout[128 * jt:128 * (jt + 1), tcol0:tcol0 + 128], minb[:]))
        _ser_build(Sr, block, sems)
    return nc


def build_EC():
    nc = _mk()
    hmidT = nc.dram_tensor("hmidT", [D, T], F32, kind="ExternalInput").ap()
    moeT = nc.dram_tensor("moeT", [D, T], F32, kind="ExternalInput").ap()
    sT = nc.dram_tensor("sT", [D, T], F32, kind="ExternalInput").ap()
    x = nc.dram_tensor("x", [T, D], F32, kind="ExternalInput").ap()
    eyeH = nc.dram_tensor("eye", [128, 128], F32, kind="ExternalInput").ap()
    onesH = nc.dram_tensor("ones", [128, 128], F32, kind="ExternalInput").ap()
    x1T = nc.dram_tensor("x1T", [D, T], F32, kind="ExternalOutput").ap()
    xn1T = nc.dram_tensor("xn1T", [D, T], F32, kind="ExternalOutput").ap()

    with ExitStack() as ctx:
        sb = lambda n, s: ctx.enter_context(nc.sbuf_tensor(n, s, F32))
        ps = lambda n, s: ctx.enter_context(nc.psum_tensor(n, s, F32))
        xrow = sb("xrow", [128, D])
        x1 = [sb(f"x1_{j}", [128, 512]) for j in range(8)]
        hbuf = sb("hbuf", [128, 128]); mbuf = sb("mbuf", [128, 128]); sbuf_ = sb("sbuf_", [128, 128])
        eye_sb = sb("eyes", [128, 128]); ones_sb = sb("oness", [128, 128])
        sqb = sb("sqb", [128, 512])
        murow = sb("murow", [1, 512]); ssrow = sb("ssrow", [1, 512]); vrow = sb("vrow", [1, 512])
        mub = sb("mub", [128, 512]); rsb = sb("rsb", [128, 512])
        evb = sb("evb", [128, 512])
        cst = sb("cst", [128, 2])
        pT = ps("pT", [128, 512]); pr = ps("pr", [1, 512]); pr2 = ps("pr2", [1, 512])
        psm = ps("psm", [128, 128])
        sems = [ctx.enter_context(nc.semaphore(f"s{i}")) for i in range(8)]
        block = ctx.enter_context(nc.Block())
        Sr = Ser(); E, Dm = Sr.emit, Sr.dma

        Dm("sync", lambda q: q.dma_start(eye_sb[:], eyeH[:, :]))
        Dm("sync", lambda q: q.dma_start(ones_sb[:], onesH[:, :]))
        E("vector", lambda q: q.memset(cst[:, 0:1], 1e-5))
        for ts in range(4):
            for tc in range(4):
                t0 = 512 * ts + 128 * tc
                Dm("sync", lambda q, t0=t0: q.dma_start(xrow[:], x[t0:t0 + 128, :]))
                for j in range(8):
                    E("tensor", lambda q, j=j: q.transpose(psm[:, 0:128], xrow[:, 128 * j:128 * (j + 1)], eye_sb[:]))
                    Dm("sync", lambda q, j=j, t0=t0: q.dma_start(hbuf[:], hmidT[128 * j:128 * (j + 1), t0:t0 + 128]))
                    Dm("sync", lambda q, j=j, t0=t0: q.dma_start(mbuf[:], moeT[128 * j:128 * (j + 1), t0:t0 + 128]))
                    Dm("sync", lambda q, j=j, t0=t0: q.dma_start(sbuf_[:], sT[128 * j:128 * (j + 1), t0:t0 + 128]))
                    E("vector", lambda q: q.tensor_tensor(hbuf[:], hbuf[:], mbuf[:], ALU.add))
                    E("scalar", lambda q: q.activation(hbuf[:], hbuf[:], AF.Silu))
                    E("vector", lambda q: q.tensor_tensor(hbuf[:], hbuf[:], sbuf_[:], ALU.mult))
                    E("vector", lambda q, j=j, tc=tc: q.tensor_tensor(x1[j][:, 128 * tc:128 * (tc + 1)], hbuf[:], psm[:, 0:128], ALU.add))
                    Dm("gpsimd", lambda q, j=j, t0=t0, tc=tc: q.dma_start(x1T[128 * j:128 * (j + 1), t0:t0 + 128], x1[j][:, 128 * tc:128 * (tc + 1)]))
            for j in range(8):
                E("tensor", lambda q, j=j: q.matmul(pr[:], ones_sb[:, 0:1], x1[j][:], start=(j == 0), stop=(j == 7)))
            for j in range(8):
                E("scalar", lambda q, j=j: q.activation(sqb[:], x1[j][:], AF.Square))
                E("tensor", lambda q, j=j: q.matmul(pr2[:], ones_sb[:, 0:1], sqb[:], start=(j == 0), stop=(j == 7)))
            E("scalar", lambda q: q.activation(murow[:], pr[:], AF.Copy, scale=1.0 / D))
            E("scalar", lambda q: q.activation(ssrow[:], pr2[:], AF.Copy, scale=1.0 / D))
            E("scalar", lambda q: q.activation(vrow[:], murow[:], AF.Square))
            E("vector", lambda q: q.tensor_tensor(vrow[:], ssrow[:], vrow[:], ALU.subtract))
            E("scalar", lambda q: q.activation(vrow[:], vrow[:], AF.Sqrt, bias=cst[0:1, 0:1]))
            E("vector", lambda q: q.reciprocal(vrow[:], vrow[:]))
            E("tensor", lambda q: q.matmul(pT[:, 0:512], ones_sb[0:1, :], murow[:], start=True, stop=True))
            E("scalar", lambda q: q.copy(mub[:], pT[:, 0:512]))
            E("tensor", lambda q: q.matmul(pT[:, 0:512], ones_sb[0:1, :], vrow[:], start=True, stop=True))
            E("scalar", lambda q: q.copy(rsb[:], pT[:, 0:512]))
            for j in range(8):
                E("vector", lambda q, j=j: q.tensor_tensor(evb[:], x1[j][:], mub[:], ALU.subtract))
                E("vector", lambda q: q.tensor_tensor(evb[:], evb[:], rsb[:], ALU.mult))
                Dm("gpsimd", lambda q, j=j, ts=ts: q.dma_start(xn1T[128 * j:128 * (j + 1), 512 * ts:512 * (ts + 1)], evb[:]))
        _ser_build(Sr, block, sems)
    return nc


def build_F():
    nc = _mk()
    xn1T = nc.dram_tensor("xn1T", [D, T], F32, kind="ExternalInput").ap()
    names = ["q", "k", "v"]
    Ws, bs, As, Bs = {}, {}, {}, {}
    for n in names:
        Ws[n] = nc.dram_tensor("W" + n, [D, D], F32, kind="ExternalInput").ap()
        bs[n] = nc.dram_tensor("b" + n, [D], F32, kind="ExternalInput").ap()
        As[n] = nc.dram_tensor("A" + n, [8, D], F32, kind="ExternalInput").ap()
        Bs[n] = nc.dram_tensor("B" + n, [D, 8], F32, kind="ExternalInput").ap()
    U1 = nc.dram_tensor("U1", [D, 256], F32, kind="ExternalInput").ap()
    V1 = nc.dram_tensor("V1", [256, DFF], F32, kind="ExternalInput").ap()
    b1 = nc.dram_tensor("fb1", [DFF], F32, kind="ExternalInput").ap()
    U2 = nc.dram_tensor("U2", [DFF, 256], F32, kind="ExternalInput").ap()
    V2 = nc.dram_tensor("V2", [256, D], F32, kind="ExternalInput").ap()
    b2 = nc.dram_tensor("fb2", [D], F32, kind="ExternalInput").ap()
    eyeH = nc.dram_tensor("eye", [128, 128], F32, kind="ExternalInput").ap()
    onesH = nc.dram_tensor("ones", [128, 128], F32, kind="ExternalInput").ap()
    QTo = nc.dram_tensor("QT", [D, T], F32, kind="ExternalOutput").ap()
    KTo = nc.dram_tensor("KT", [D, T], F32, kind="ExternalOutput").ap()
    Vo = nc.dram_tensor("V", [T, D], F32, kind="ExternalOutput").ap()
    g1To = nc.dram_tensor("g1T", [D, T], F32, kind="ExternalOutput").ap()

    with ExitStack() as ctx:
        sb = lambda n, s: ctx.enter_context(nc.sbuf_tensor(n, s, F32))
        ps = lambda n, s: ctx.enter_context(nc.psum_tensor(n, s, F32))
        xn = [sb(f"xn{j}", [128, T]) for j in range(8)]
        wrow = sb("wrow", [128, D])
        wT = [sb(f"wT{j}", [128, D]) for j in range(8)]
        aT = [sb(f"aT{j}", [128, 8]) for j in range(8)]
        bT = sb("bT", [8, D])
        asb = sb("asb", [8, D])
        bsb = sb("bsb", [128, 8])
        uT = sb("uT", [8, T])
        bcol = sb("bcol", [128, 8])
        brow = sb("brow", [1, D])
        eye_sb = sb("eyes", [128, 128]); ones_sb = sb("oness", [128, 128])
        evb = sb("evb", [128, 512])
        PT = [sb(f"PT{j}", [128, 256]) for j in range(2)]
        HT = [sb(f"HT{j}", [128, 256]) for j in range(32)]
        QT2 = [sb(f"QT2{j}", [128, 256]) for j in range(2)]
        xg = sb("xg", [128, 256]); scr = sb("scr", [128, 256]); scr2 = sb("scr2", [128, 256])
        b1c = sb("b1c", [128, 32]); b2c = sb("b2c", [128, 8])
        pT = ps("pT", [128, 512]); psm = ps("psm", [128, 128]); pu = ps("pu", [8, 512])
        sems = [ctx.enter_context(nc.semaphore(f"s{i}")) for i in range(8)]
        block = ctx.enter_context(nc.Block())
        Sr = Ser(); E, Dm = Sr.emit, Sr.dma

        Dm("sync", lambda q: q.dma_start(eye_sb[:], eyeH[:, :]))
        Dm("sync", lambda q: q.dma_start(ones_sb[:], onesH[:, :]))
        for j in range(8):
            Dm("sync", lambda q, j=j: q.dma_start(xn[j][:], xn1T[128 * j:128 * (j + 1), :]))

        def lora_pre(n):
            Dm("sync", lambda q, n=n: q.dma_start(asb[:], As[n][:, :]))
            for j in range(8):
                E("tensor", lambda q, j=j: q.transpose(psm[0:128, 0:8], asb[0:8, 128 * j:128 * (j + 1)], eye_sb[0:8, 0:8]))
                E("scalar", lambda q, j=j: q.copy(aT[j][:], psm[0:128, 0:8]))
            for j in range(8):
                Dm("sync", lambda q, n=n, j=j: q.dma_start(bsb[:], Bs[n][128 * j:128 * (j + 1), :]))
                E("tensor", lambda q: q.transpose(psm[0:8, 0:128], bsb[:], eye_sb[:]))
                E("scalar", lambda q, j=j: q.copy(bT[0:8, 128 * j:128 * (j + 1)], psm[0:8, 0:128]))
            for ts in range(4):
                for j in range(8):
                    E("tensor", lambda q, j=j, ts=ts: q.matmul(pu[:, 0:512], aT[j][:], xn[j][:, 512 * ts:512 * (ts + 1)], start=(j == 0), stop=(j == 7)))
                E("scalar", lambda q, ts=ts: q.activation(uT[0:8, 512 * ts:512 * (ts + 1)], pu[:, 0:512], AF.Copy, scale=0.125))

        for n, outT in (("q", QTo), ("k", KTo)):
            _wtrans(Sr, Ws[n], wrow, wT, pT, eye_sb)
            lora_pre(n)
            for j in range(8):
                _colload(Sr, bcol[:, j:j + 1], bs[n], j)
            for oc in range(8):
                for ts in range(4):
                    for j in range(8):
                        E("tensor", lambda q, oc=oc, ts=ts, j=j: q.matmul(
                            pT[:, 0:512], wT[j][:, 128 * oc:128 * (oc + 1)], xn[j][:, 512 * ts:512 * (ts + 1)], start=(j == 0), stop=False))
                    E("tensor", lambda q, oc=oc, ts=ts: q.matmul(
                        pT[:, 0:512], bT[0:8, 128 * oc:128 * (oc + 1)], uT[0:8, 512 * ts:512 * (ts + 1)], start=False, stop=True))
                    E("vector", lambda q, oc=oc: q.tensor_scalar(evb[:], pT[:, 0:512], bcol[:, oc:oc + 1], None, ALU.add))
                    Dm("gpsimd", lambda q, oc=oc, ts=ts, outT=outT: q.dma_start(
                        outT[128 * oc:128 * (oc + 1), 512 * ts:512 * (ts + 1)], evb[:]))
        _wtrans(Sr, Ws["v"], wrow, wT, pT, eye_sb)
        lora_pre("v")
        Dm("sync", lambda q: q.dma_start(brow[:], bs["v"].rearrange("(a b) -> a b", a=1)))
        for tt in range(16):
            for half in range(2):
                for j in range(8):
                    E("tensor", lambda q, tt=tt, half=half, j=j: q.matmul(
                        pT[:, 0:512], xn[j][:, 128 * tt:128 * (tt + 1)], wT[j][:, 512 * half:512 * (half + 1)], start=(j == 0), stop=False))
                E("tensor", lambda q, half=half: q.matmul(pT[:, 0:512], ones_sb[0:1, :], brow[0:1, 512 * half:512 * (half + 1)], start=False, stop=False))
                E("tensor", lambda q, tt=tt, half=half: q.matmul(
                    pT[:, 0:512], uT[0:8, 128 * tt:128 * (tt + 1)], bT[0:8, 512 * half:512 * (half + 1)], start=False, stop=True))
                E("scalar", lambda q: q.copy(evb[:], pT[:, 0:512]))
                Dm("gpsimd", lambda q, tt=tt, half=half: q.dma_start(
                    Vo[128 * tt:128 * (tt + 1), 512 * half:512 * (half + 1)], evb[:]))
        for fc in range(32):
            _colload(Sr, b1c[:, fc:fc + 1], b1, fc)
        for j in range(8):
            _colload(Sr, b2c[:, j:j + 1], b2, j)
        for ts8 in range(8):
            sl = slice(256 * ts8, 256 * (ts8 + 1))
            for rc in range(2):
                for j in range(8):
                    Dm("sync", lambda q, rc=rc, j=j: q.dma_start(wrow[:, 0:128], U1[128 * j:128 * (j + 1), 128 * rc:128 * (rc + 1)]))
                    E("tensor", lambda q, j=j, sl=sl: q.matmul(pT[:, 0:256], wrow[:, 0:128], xn[j][:, sl], start=(j == 0), stop=(j == 7)))
                E("scalar", lambda q, rc=rc: q.copy(PT[rc][:], pT[:, 0:256]))
            for fc in range(32):
                for rc in range(2):
                    Dm("sync", lambda q, rc=rc, fc=fc: q.dma_start(wrow[:, 0:128], V1[128 * rc:128 * (rc + 1), 128 * fc:128 * (fc + 1)]))
                    E("tensor", lambda q, rc=rc: q.matmul(pT[:, 0:256], wrow[:, 0:128], PT[rc][:], start=(rc == 0), stop=(rc == 1)))
                E("vector", lambda q, fc=fc: q.tensor_scalar(xg[:], pT[:, 0:256], b1c[:, fc:fc + 1], None, ALU.add))
                E("scalar", lambda q: q.activation(scr[:], xg[:], AF.Square))
                E("vector", lambda q: q.tensor_scalar(scr[:], scr[:], 0.044715, 1.0, ALU.mult, ALU.add))
                E("vector", lambda q: q.tensor_tensor(scr[:], scr[:], xg[:], ALU.mult))
                E("scalar", lambda q: q.activation(scr[:], scr[:], AF.Tanh, scale=GELU_C))
                E("vector", lambda q: q.tensor_scalar(scr[:], scr[:], 0.5, 0.5, ALU.mult, ALU.add))
                E("vector", lambda q, fc=fc: q.tensor_tensor(HT[fc][:], scr[:], xg[:], ALU.mult))
            for rc in range(2):
                for fc in range(32):
                    Dm("sync", lambda q, rc=rc, fc=fc: q.dma_start(wrow[:, 0:128], U2[128 * fc:128 * (fc + 1), 128 * rc:128 * (rc + 1)]))
                    E("tensor", lambda q, fc=fc: q.matmul(pT[:, 0:256], wrow[:, 0:128], HT[fc][:], start=(fc == 0), stop=(fc == 31)))
                E("scalar", lambda q, rc=rc: q.copy(QT2[rc][:], pT[:, 0:256]))
            for oc in range(8):
                for rc in range(2):
                    Dm("sync", lambda q, rc=rc, oc=oc: q.dma_start(wrow[:, 0:128], V2[128 * rc:128 * (rc + 1), 128 * oc:128 * (oc + 1)]))
                    E("tensor", lambda q, rc=rc: q.matmul(pT[:, 0:256], wrow[:, 0:128], QT2[rc][:], start=(rc == 0), stop=(rc == 1)))
                E("scalar", lambda q, oc=oc: q.activation(scr2[:], pT[:, 0:256], AF.Silu, bias=b2c[:, oc:oc + 1]))
                Dm("gpsimd", lambda q, oc=oc, sl=sl: q.dma_start(g1To[128 * oc:128 * (oc + 1), sl], scr2[:]))
        _ser_build(Sr, block, sems)
    return nc


def build_G():
    nc = _mk()
    QTi = nc.dram_tensor("QT", [D, T], F32, kind="ExternalInput").ap()
    KTb = nc.dram_tensor("KTb", [D, S], F32, kind="ExternalInput").ap()
    Vb = nc.dram_tensor("Vb", [S, D], F32, kind="ExternalInput").ap()
    Wo = nc.dram_tensor("Wo", [D, D], F32, kind="ExternalInput").ap()
    bo = nc.dram_tensor("bo", [D], F32, kind="ExternalInput").ap()
    Ao = nc.dram_tensor("Ao", [8, D], F32, kind="ExternalInput").ap()
    Bo = nc.dram_tensor("Bo", [D, 8], F32, kind="ExternalInput").ap()
    rez = nc.dram_tensor("rez", [1], F32, kind="ExternalInput").ap()
    x1T = nc.dram_tensor("x1T", [D, T], F32, kind="ExternalInput").ap()
    xn1T = nc.dram_tensor("xn1T", [D, T], F32, kind="ExternalInput").ap()
    g1T = nc.dram_tensor("g1T", [D, T], F32, kind="ExternalInput").ap()
    iotaH = nc.dram_tensor("iota2d", [128, 64], F32, kind="ExternalInput").ap()
    eyeH = nc.dram_tensor("eye", [128, 128], F32, kind="ExternalInput").ap()
    onesH = nc.dram_tensor("ones", [128, 128], F32, kind="ExternalInput").ap()
    out = nc.dram_tensor("out", [T, D], F32, kind="ExternalOutput").ap()
    hao = nc.dram_tensor("hao", [D, T], F32, kind="ExternalOutput").ap()
    s1o = nc.dram_tensor("s1o", [D, T], F32, kind="ExternalOutput").ap()

    with ExitStack() as ctx:
        sb = lambda n, s: ctx.enter_context(nc.sbuf_tensor(n, s, F32))
        ps = lambda n, s: ctx.enter_context(nc.psum_tensor(n, s, F32))
        KTh = sb("KTh", [64, S])
        Vp = sb("Vp", [128, 65 * 64])
        qh = sb("qh", [64, T])
        ha = [sb(f"ha{j}", [128, T]) for j in range(8)]
        wrow = sb("wrow", [128, D])
        wT = [sb(f"wT{j}", [128, D]) for j in range(8)]
        aT = [sb(f"aT{j}", [128, 8]) for j in range(8)]
        bT = sb("bT", [8, D]); asb = sb("asb", [8, D]); bsb = sb("bsb", [128, 8])
        uT = sb("uT", [8, T])
        iot = sb("iot", [128, 64]); bt = sb("bt", [128, 64])
        eye_sb = sb("eyes", [128, 128]); ones_sb = sb("oness", [128, 128])
        attnT = sb("attnT", [128, 512]); scr = sb("scr", [128, 512])
        numsb = sb("numsb", [64, 512]); rrow = sb("rrow", [1, 512])
        kmax = sb("kmax", [1, 1]); qmax = sb("qmax", [1, 1]); smax = sb("smax", [1, 1])
        m11 = sb("m11", [1, 1]); scol = sb("scol", [128, 1])
        bocol = sb("bocol", [128, 8]); rcol = sb("rcol", [128, 1])
        sjt = sb("sjt", [128, 512]); gjt = sb("gjt", [128, 512]); xjt = sb("xjt", [128, 512])
        pqk = ps("pqk", [128, 512]); pav = ps("pav", [65, 512]); pbc = ps("pbc", [64, 512])
        prow = ps("prow", [1, 512]); psm = ps("psm", [128, 128]); pu = ps("pu", [8, 512])
        pc = ps("pc", [128, 1])
        sems = [ctx.enter_context(nc.semaphore(f"s{i}")) for i in range(8)]
        block = ctx.enter_context(nc.Block())
        Sr = Ser(); E, Dm = Sr.emit, Sr.dma

        Dm("sync", lambda q: q.dma_start(eye_sb[:], eyeH[:, :]))
        Dm("sync", lambda q: q.dma_start(ones_sb[:], onesH[:, :]))
        Dm("sync", lambda q: q.dma_start(iot[:], iotaH[:, :]))

        for h in range(NH):
            slope = 0.5 ** h
            Dm("sync", lambda q, h=h: q.dma_start(KTh[:], KTb[64 * h:64 * (h + 1), :]))
            Dm("sync", lambda q, h=h: q.dma_start(qh[:], QTi[64 * h:64 * (h + 1), :]))
            E("vector", lambda q: q.memset(Vp[:], 1.0))
            for kc in range(64):
                Dm("sync", lambda q, h=h, kc=kc: q.dma_start(Vp[:, 65 * kc:65 * kc + 64], Vb[128 * kc:128 * (kc + 1), 64 * h:64 * (h + 1)]))
            E("vector", lambda q: q.memset(kmax[:], 0.0))
            E("vector", lambda q: q.memset(qmax[:], 0.0))
            for ks in range(16):
                E("scalar", lambda q, ks=ks: q.activation(scr[0:64, 0:512], KTh[:, 512 * ks:512 * (ks + 1)], AF.Square))
                E("tensor", lambda q: q.matmul(prow[:], ones_sb[0:64, 0:1], scr[0:64, 0:512], start=True, stop=True))
                E("vector", lambda q: q.tensor_reduce(m11[:], prow[:], AX.X, ALU.max))
                E("vector", lambda q: q.tensor_tensor(kmax[:], kmax[:], m11[:], ALU.max))
            for ks in range(4):
                E("scalar", lambda q, ks=ks: q.activation(scr[0:64, 0:512], qh[:, 512 * ks:512 * (ks + 1)], AF.Square))
                E("tensor", lambda q: q.matmul(prow[:], ones_sb[0:64, 0:1], scr[0:64, 0:512], start=True, stop=True))
                E("vector", lambda q: q.tensor_reduce(m11[:], prow[:], AX.X, ALU.max))
                E("vector", lambda q: q.tensor_tensor(qmax[:], qmax[:], m11[:], ALU.max))
            E("scalar", lambda q: q.activation(kmax[:], kmax[:], AF.Sqrt))
            E("scalar", lambda q: q.activation(qmax[:], qmax[:], AF.Sqrt))
            E("vector", lambda q: q.tensor_tensor(smax[:], kmax[:], qmax[:], ALU.mult))
            E("vector", lambda q: q.tensor_scalar(smax[:], smax[:], 0.125, None, ALU.mult))
            E("tensor", lambda q: q.matmul(pc[:], ones_sb[0:1, :], smax[:], start=True, stop=True))
            E("scalar", lambda q: q.copy(scol[:], pc[:]))
            E("vector", lambda q, slope=slope: q.tensor_scalar(bt[:], iot[:], -8191.0, slope, ALU.add, ALU.mult))
            E("vector", lambda q: q.tensor_scalar(bt[:], bt[:], scol[:], None, ALU.subtract))
            for qt_i in range(4):
                for kc in range(64):
                    E("tensor", lambda q, kc=kc, qt_i=qt_i: q.matmul(
                        pqk[:, 0:512], KTh[:, 128 * kc:128 * (kc + 1)], qh[:, 512 * qt_i:512 * (qt_i + 1)], start=True, stop=True))
                    E("scalar", lambda q, kc=kc: q.activation(attnT[:], pqk[:, 0:512], AF.Exp, bias=bt[:, kc:kc + 1], scale=0.125))
                    E("tensor", lambda q, kc=kc: q.matmul(pav[:, 0:512], Vp[:, 65 * kc:65 * (kc + 1)], attnT[:], start=(kc == 0), stop=(kc == 63)))
                E("vector", lambda q: q.reciprocal(rrow[:], pav[64:65, :]))
                E("tensor", lambda q: q.matmul(pbc[:, 0:512], ones_sb[0:1, 0:64], rrow[:], start=True, stop=True))
                E("scalar", lambda q: q.copy(numsb[:], pav[0:64, :]))
                hb = 64 * (h % 2)
                E("vector", lambda q, h=h, hb=hb, qt_i=qt_i: q.tensor_tensor(
                    ha[h // 2][hb:hb + 64, 512 * qt_i:512 * (qt_i + 1)], numsb[:], pbc[:, 0:512], ALU.mult))
        for j in range(8):
            Dm("gpsimd", lambda q, j=j: q.dma_start(hao[128 * j:128 * (j + 1), :], ha[j][:]))
        _wtrans(Sr, Wo, wrow, wT, pqk, eye_sb)
        Dm("sync", lambda q: q.dma_start(asb[:], Ao[:, :]))
        for j in range(8):
            E("tensor", lambda q, j=j: q.transpose(psm[0:128, 0:8], asb[0:8, 128 * j:128 * (j + 1)], eye_sb[0:8, 0:8]))
            E("scalar", lambda q, j=j: q.copy(aT[j][:], psm[0:128, 0:8]))
        for j in range(8):
            Dm("sync", lambda q, j=j: q.dma_start(bsb[:], Bo[128 * j:128 * (j + 1), :]))
            E("tensor", lambda q: q.transpose(psm[0:8, 0:128], bsb[:], eye_sb[:]))
            E("scalar", lambda q, j=j: q.copy(bT[0:8, 128 * j:128 * (j + 1)], psm[0:8, 0:128]))
        for ts in range(4):
            for j in range(8):
                E("tensor", lambda q, j=j, ts=ts: q.matmul(pu[:, 0:512], aT[j][:], ha[j][:, 512 * ts:512 * (ts + 1)], start=(j == 0), stop=(j == 7)))
            E("scalar", lambda q, ts=ts: q.activation(uT[0:8, 512 * ts:512 * (ts + 1)], pu[:, 0:512], AF.Copy, scale=0.125))
        for j in range(8):
            _colload(Sr, bocol[:, j:j + 1], bo, j)
        Dm("sync", lambda q: q.dma_start(kmax[:], rez.rearrange("(a b) -> a b", a=1)))
        E("tensor", lambda q: q.matmul(pc[:], ones_sb[0:1, :], kmax[:], start=True, stop=True))
        E("scalar", lambda q: q.copy(rcol[:], pc[:]))
        for ts in range(4):
            sl = slice(512 * ts, 512 * (ts + 1))
            for oc in range(8):
                for j in range(8):
                    E("tensor", lambda q, oc=oc, j=j, sl=sl: q.matmul(
                        pqk[:, 0:512], wT[j][:, 128 * oc:128 * (oc + 1)], ha[j][:, sl], start=(j == 0), stop=False))
                E("tensor", lambda q, oc=oc, sl=sl: q.matmul(
                    pqk[:, 0:512], bT[0:8, 128 * oc:128 * (oc + 1)], uT[0:8, sl], start=False, stop=True))
                E("vector", lambda q, oc=oc: q.tensor_scalar(scr[:, 0:512], pqk[:, 0:512], bocol[:, oc:oc + 1], None, ALU.add))
                Dm("sync", lambda q, oc=oc, sl=sl: q.dma_start(xjt[:], xn1T[128 * oc:128 * (oc + 1), sl]))
                E("vector", lambda q: q.scalar_tensor_tensor(sjt[:], scr[:, 0:512], rcol[:], xjt[:], ALU.mult, ALU.add))
                Dm("gpsimd", lambda q, oc=oc, sl=sl: q.dma_start(s1o[128 * oc:128 * (oc + 1), sl], sjt[:]))
                Dm("sync", lambda q, oc=oc, sl=sl: q.dma_start(gjt[:], g1T[128 * oc:128 * (oc + 1), sl]))
                E("vector", lambda q: q.tensor_tensor(sjt[:], sjt[:], gjt[:], ALU.mult))
                Dm("sync", lambda q, oc=oc, sl=sl: q.dma_start(xjt[:], x1T[128 * oc:128 * (oc + 1), sl]))
                E("vector", lambda q: q.tensor_tensor(sjt[:], sjt[:], xjt[:], ALU.add))
                for sub in range(4):
                    E("tensor", lambda q, sub=sub: q.transpose(psm[:, 0:128], sjt[:, 128 * sub:128 * (sub + 1)], eye_sb[:]))
                    E("scalar", lambda q: q.copy(scr[:, 0:128], psm[:, 0:128]))
                    Dm("gpsimd", lambda q, oc=oc, ts=ts, sub=sub: q.dma_start(
                        out[512 * ts + 128 * sub:512 * ts + 128 * (sub + 1), 128 * oc:128 * (oc + 1)], scr[:, 0:128]))
        _ser_build(Sr, block, sems)
    return nc


_PROGS = {}
_DBG = {}


def _prog(name, fn, *args):
    key = (name,) + args
    if key not in _PROGS:
        _PROGS[key] = fn(*args)
    return _PROGS[key]


def kernel(**inputs):
    C = _consts()
    f32 = lambda a: np.ascontiguousarray(np.asarray(a, dtype=np.float32))
    x_full = f32(inputs["L0_attn_x"])
    cores = [(i, i // 4, (i % 4) * T) for i in range(8)]
    xs = [f32(x_full[b, r0:r0 + T]) for (i, b, r0) in cores]

    ncA = _prog("A", build_A)
    inA = [dict(x=xs[i], w1=f32(inputs["L0_ffn_norm1_w"]),
                cosv=f32(C["cosT"][r0:r0 + T]), sinv=f32(C["sinT"][r0:r0 + T]),
                eye=C["eye"], ones=C["ones"]) for (i, b, r0) in cores]
    rA = run_bass_kernel_spmd(ncA, inA, core_ids=list(range(8))).results
    xn0T = [rA[i]["xn0T"] for i in range(8)]
    xr1T = [rA[i]["xr1T"] for i in range(8)]

    ncB = _prog("B", build_B)
    inB = [dict(xn0T=xn0T[i], Wq=f32(inputs["L0_attn_Wq"]), Wv=f32(inputs["L0_attn_Wv"]),
                Wg=f32(inputs["L0_attn_Wg"]), dec=f32(inputs["L0_attn_decay"]),
                Hmap=C["Hmap"], eye=C["eye"]) for (i, b, r0) in cores]
    rB = run_bass_kernel_spmd(ncB, inB, core_ids=list(range(8))).results
    qT = [rB[i]["qT"] for i in range(8)]
    vT = [rB[i]["vT"] for i in range(8)]
    gT = [rB[i]["gT"] for i in range(8)]
    Ev = [rB[i]["E"] for i in range(8)]

    ncC = _prog("C", build_C)
    inC = []
    for (i, b, r0) in cores:
        ib = i % 4
        z = np.zeros(D, np.float32)
        ep = [Ev[4 * b + ib - m] if ib - m >= 0 else z for m in (1, 2, 3)]
        inC.append(dict(vT=vT[i], qT=qT[i], gT=gT[i], Eprev=f32(np.stack(ep)),
                        dec=f32(inputs["L0_attn_decay"]), Hmap=C["Hmap"],
                        gam=f32(inputs["L0_attn_ln_gamma"]), bet=f32(inputs["L0_attn_ln_beta"]),
                        Wo=f32(inputs["L0_attn_Wo"]), eye=C["eye"], ones=C["ones"]))
    rC = run_bass_kernel_spmd(ncC, inC, core_ids=list(range(8))).results
    sT = [rC[i]["sT"] for i in range(8)]

    hmidT = [None] * 8
    xr2T = [None] * 8
    combT = [None] * 8
    for cls in range(4):
        ncD = _prog("D", build_D, cls)
        ids = [cls, 4 + cls]
        inD = []
        for i in ids:
            b = i // 4
            xrB = np.concatenate([xr1T[4 * b + j] for j in range(cls + 1)], axis=1)
            inD.append(dict(xrB=f32(xrB), xn0T=xn0T[i],
                            Wq=f32(inputs["L0_ffn_Wq"]), Wk=f32(inputs["L0_ffn_Wk"]),
                            Wv=f32(inputs["L0_ffn_Wv"]), Wo=f32(inputs["L0_ffn_Wo"]),
                            w2=f32(inputs["L0_ffn_norm2_w"]), rw=f32(inputs["L0_ffn_router_w"]),
                            rb=f32(inputs["L0_ffn_router_b"]), KQ=C["KQ"],
                            eye=C["eye"], ones=C["ones"]))
        rD = run_bass_kernel_spmd(ncD, inD, core_ids=ids).results
        for k, i in enumerate(ids):
            hmidT[i] = rD[k]["hmidT"]
            xr2T[i] = rD[k]["xr2T"]
            combT[i] = rD[k]["combT"]

    ncE = _prog("E", build_E)
    moe = [np.zeros((D, T), np.float32) for _ in range(8)]
    for eg in range(4):
        inE = []
        for (i, b, r0) in cores:
            inE.append(dict(xr2T=xr2T[i], comb2=f32(combT[i][2 * eg:2 * eg + 2]),
                            W1=f32(inputs["L0_ffn_W1"][2 * eg:2 * eg + 2]),
                            W2=f32(inputs["L0_ffn_W2"][2 * eg:2 * eg + 2]),
                            W3=f32(inputs["L0_ffn_W3"][2 * eg:2 * eg + 2]),
                            b1=f32(inputs["L0_ffn_b1"][2 * eg:2 * eg + 2]),
                            b2=f32(inputs["L0_ffn_b2"][2 * eg:2 * eg + 2]),
                            b3=f32(inputs["L0_ffn_b3"][2 * eg:2 * eg + 2]),
                            ones=C["ones"], eye=C["eye"], moein=moe[i]))
        rE = run_bass_kernel_spmd(ncE, inE, core_ids=list(range(8))).results
        moe = [rE[i]["moeout"] for i in range(8)]

    ncEC = _prog("EC", build_EC)
    inEC = [dict(hmidT=hmidT[i], moeT=moe[i], sT=sT[i], x=xs[i],
                 eye=C["eye"], ones=C["ones"]) for (i, b, r0) in cores]
    rEC = run_bass_kernel_spmd(ncEC, inEC, core_ids=list(range(8))).results
    x1T = [rEC[i]["x1T"] for i in range(8)]
    xn1T = [rEC[i]["xn1T"] for i in range(8)]

    ncF = _prog("F", build_F)
    inF = []
    for (i, b, r0) in cores:
        d = dict(xn1T=xn1T[i], eye=C["eye"], ones=C["ones"],
                 U1=f32(inputs["L1_ffn_U1"]), V1=f32(inputs["L1_ffn_V1"]),
                 fb1=f32(inputs["L1_ffn_b1"]), U2=f32(inputs["L1_ffn_U2"]),
                 V2=f32(inputs["L1_ffn_V2"]), fb2=f32(inputs["L1_ffn_b2"]))
        for n in ("q", "k", "v"):
            d["W" + n] = f32(inputs[f"L1_attn_W{n}"])
            d["b" + n] = f32(inputs[f"L1_attn_b{n}"])
            d["A" + n] = f32(inputs[f"L1_attn_A{n}"])
            d["B" + n] = f32(inputs[f"L1_attn_B{n}"])
        inF.append(d)
    rF = run_bass_kernel_spmd(ncF, inF, core_ids=list(range(8))).results
    QT = [rF[i]["QT"] for i in range(8)]
    KT = [rF[i]["KT"] for i in range(8)]
    V = [rF[i]["V"] for i in range(8)]
    g1T = [rF[i]["g1T"] for i in range(8)]

    ncG = _prog("G", build_G)
    inG = []
    for (i, b, r0) in cores:
        KTb = np.concatenate([KT[4 * b + j] for j in range(4)], axis=1)
        Vb = np.concatenate([V[4 * b + j] for j in range(4)], axis=0)
        inG.append(dict(QT=QT[i], KTb=f32(KTb), Vb=f32(Vb),
                        Wo=f32(inputs["L1_attn_Wo"]), bo=f32(inputs["L1_attn_bo"]),
                        Ao=f32(inputs["L1_attn_Ao"]), Bo=f32(inputs["L1_attn_Bo"]),
                        rez=f32(inputs["L1_attn_rezero"]), x1T=x1T[i], xn1T=xn1T[i],
                        g1T=g1T[i], iota2d=C["iota2d"], eye=C["eye"], ones=C["ones"]))
    rG = run_bass_kernel_spmd(ncG, inG, core_ids=list(range(8))).results
    _DBG.update(hao=[rG[i]["hao"] for i in range(8)], s1o=[rG[i]["s1o"] for i in range(8)],xn0T=xn0T, xr1T=xr1T, qT=qT, vT=vT, gT=gT, Ev=Ev, sT=sT,
                hmidT=hmidT, xr2T=xr2T, combT=combT, moe=moe, x1T=x1T,
                xn1T=xn1T, QT=QT, KT=KT, V=V, g1T=g1T)
    outp = np.empty((B, S, D), np.float32)
    for (i, b, r0) in cores:
        outp[b, r0:r0 + T] = rG[i]["out"]
    return outp
